# version 03
# baseline (speedup 1.0000x reference)
import contextlib
import numpy as np
import ml_dtypes
import concourse.bass as bass
import concourse.mybir as mybir
from concourse.bass_utils import run_bass_kernel_spmd

F32 = mybir.dt.float32
BF16 = mybir.dt.bfloat16
FP8 = mybir.dt.float8e4
I32 = mybir.dt.int32
AF = mybir.ActivationFunctionType
AL = mybir.AluOpType
ET = mybir.EngineType
NBF = ml_dtypes.bfloat16

S, D, NH, NKV, HDIM = 8192, 4096, 16, 4, 64
LAT, KVD = 1024, 256
CH, HALO = 512, 16
CW = CH + HALO
NCHUNK = S // CH
M_A = 10
NSLAB = 8
NOWN = 5
DT = D // 128
QKV_C = 1600
NCB = 13
EPS = 1e-5
ALL5 = [ET.PE, ET.Activation, ET.DVE, ET.Pool, ET.SP]

DBG_STOP = None
DBG_CONVKV = 99
DBG_DUMP = False


def _role_tables():
    a_band = list(range(M_A))
    b_band = list(range(M_A, NCHUNK))

    def split2(band):
        order = sorted(band, key=lambda c: -(c + 1))
        cap = [(len(band) + 1) // 2, len(band) // 2]
        bins, cost = [[], []], [0, 0]
        for c in order:
            i = 0 if (len(bins[0]) < cap[0] and (cost[0] <= cost[1] or len(bins[1]) >= cap[1])) else 1
            bins[i].append(c)
            cost[i] += c + 1
        return [sorted(b) for b in bins]

    a0, a1 = split2(a_band)
    b0, b1 = split2(b_band)
    return [(a0, []), (a1, []), (b0, a_band[0::2]), (b1, a_band[1::2])]


ROLE_TABLES = _role_tables()



def _colmajor_bias(b):
    C = b.shape[0]
    return np.ascontiguousarray(b.reshape(C // 128, 128).T).astype(np.float32)


def _host_consts():
    c = {}
    c["ones_bf"] = np.ones((128, 128), dtype=NBF)
    c["ones_f32"] = np.ones((128, 128), dtype=np.float32)
    P = np.zeros((128, 128), dtype=NBF)
    for i in range(128):
        P[i ^ 1, i] = 1.0
    c["swapP"] = P
    hs = np.zeros((128, 64), dtype=np.float32)
    hs[:64] = np.eye(64) * 0.125
    hs[64:] = np.eye(64) * 0.125
    c["halfsum"] = hs.astype(NBF)
    c["identity"] = np.eye(128, dtype=NBF)
    e2 = np.zeros((16, 1024), dtype=NBF)
    for m in range(8):
        e2[2 * m, 128 * m:128 * m + 64] = 1.0
        e2[2 * m + 1, 128 * m + 64:128 * m + 128] = 1.0
    c["e2sel"] = e2
    e16 = np.zeros((128, 31), dtype=NBF)
    e16[0:64, 15] = 1.0
    e16[64:128, 16] = 1.0
    c["e16"] = e16
    tm = np.zeros((128, 2048), dtype=NBF)
    for k in range(4):
        kp, e = divmod(k, 2)
        qq = np.arange(CH)[None, :]
        pp = np.arange(128)[:, None]
        tm[:, 1024 * kp + 512 * e:1024 * kp + 512 * e + 512] = (qq >= 128 * k + pp).astype(NBF)
    c["trimask"] = tm
    return c


def _rope_tables():
    inv = 1.0 / 10000.0 ** (np.arange(0, HDIM, 2, dtype=np.float64) / HDIM)
    ang = np.arange(S, dtype=np.float64)[:, None] * inv[None, :]
    dd = np.arange(HDIM)
    cos_t = np.cos(ang)[:, dd // 2]
    sin_t = np.sin(ang)[:, dd // 2] * np.where(dd % 2 == 0, -1.0, 1.0)[None, :]
    cos_t = np.ascontiguousarray(cos_t.T.astype(np.float32))
    sin_t = np.ascontiguousarray(sin_t.T.astype(np.float32))
    return np.vstack([cos_t, cos_t]), np.vstack([sin_t, sin_t])


def prep_core_inputs(inputs, core, consts_cache=[None]):
    batch = core // 4
    role = core % 4
    own, foreign = ROLE_TABLES[role]
    slabs = own + foreign

    x = np.asarray(inputs["L0_attn_x"][batch], dtype=np.float32)
    xT = x.T
    x_sl = np.zeros((NSLAB, D, CW), dtype=np.float32)
    for j, c in enumerate(slabs):
        lo = c * CH - HALO
        if lo >= 0:
            x_sl[j] = xT[:, lo:c * CH + CH]
        else:
            x_sl[j][:, HALO:] = xT[:, 0:CH]

    if consts_cache[0] is None:
        cos_t, sin_t = _rope_tables()
        consts_cache[0] = dict(
            _host_consts(), cos_t=cos_t, sin_t=sin_t,
            w_qkvT=np.ascontiguousarray(np.asarray(inputs["L0_attn_w_qkv"], dtype=np.float32).T),
            w_oT=np.ascontiguousarray(np.asarray(inputs["L0_attn_w_o"], dtype=np.float32).T),
            w_fp8T=np.ascontiguousarray(np.asarray(inputs["L0_ffn_weight"]).T),
            scale_exp=np.ascontiguousarray(
                np.repeat(np.asarray(inputs["L0_ffn_weight_scale_inv"], dtype=np.float32).T, 128, axis=1)),
            qc1T=np.ascontiguousarray(np.asarray(inputs["L0_attn_qc1_w"], dtype=np.float32).transpose(2, 1, 0)),
            qc2T=np.ascontiguousarray(np.asarray(inputs["L0_attn_qc2_w"], dtype=np.float32).transpose(2, 1, 0)),
            kc1T=np.ascontiguousarray(np.asarray(inputs["L0_attn_kc1_w"], dtype=np.float32).transpose(2, 1, 0)),
            kc2T=np.ascontiguousarray(np.asarray(inputs["L0_attn_kc2_w"], dtype=np.float32).transpose(2, 1, 0)),
            qc1b=_colmajor_bias(np.asarray(inputs["L0_attn_qc1_b"], dtype=np.float32)),
            qc2b=_colmajor_bias(np.asarray(inputs["L0_attn_qc2_b"], dtype=np.float32)),
            kc1b=_colmajor_bias(np.asarray(inputs["L0_attn_kc1_b"], dtype=np.float32)),
            kc2b=_colmajor_bias(np.asarray(inputs["L0_attn_kc2_b"], dtype=np.float32)),
            key_temp=np.asarray(inputs["L0_attn_key_temp"], dtype=np.float32).reshape(1, 1),
        )
    m = dict(consts_cache[0])
    m["x_sl"] = x_sl
    return m


def kernel(**inputs):
    nc = build_program()
    cache = [None]
    in_maps = [prep_core_inputs(inputs, c, cache) for c in range(8)]
    res = run_bass_kernel_spmd(nc, in_maps, core_ids=list(range(8)))
    out = np.zeros((2, S, D), dtype=np.float32)
    for c in range(8):
        batch = c // 4
        own, _ = ROLE_TABLES[c % 4]
        o = res.results[c]["out"]
        for j, ch in enumerate(own):
            out[batch, ch * CH:(ch + 1) * CH, :] = o[j].T
    return out



class Sem:
    def __init__(self, handle):
        self.h = handle
        self.n = 0

    def inc(self, inst, k=16):
        inst.then_inc(self.h, k)
        self.n += k
        return self.n


def chain(eng, sem, fns):
    inst = None
    for k, fn in enumerate(fns):
        if k > 0:
            eng.wait_ge(sem.h, sem.n)
        inst = fn()
        if k < len(fns) - 1:
            sem.inc(inst, 1)
    return inst


def g2(ap):
    return ap.rearrange("p (g c) -> p g c", g=2)[:, :, 0:264]


def h2(ap):
    return ap.rearrange("p (g c) -> p g c", g=2)



def build_program(detect=True):
    nc = bass.Bass("TRN2", target_bir_lowering=False, debug=False,
                   detect_race_conditions=detect)
    io = {}

    def din(name, shape, dtype):
        io[name] = nc.dram_tensor(name, shape, dtype, kind="ExternalInput").ap()

    din("x_sl", [NSLAB, D, CW], F32)
    din("w_qkvT", [D, 1536], F32)
    din("w_oT", [LAT, D], F32)
    din("w_fp8T", [D, D], FP8)
    din("scale_exp", [32, D], F32)
    din("qc1T", [3, 64, LAT], F32)
    din("qc2T", [3, LAT, LAT], F32)
    din("kc1T", [3, 64, KVD], F32)
    din("kc2T", [3, KVD, KVD], F32)
    din("qc1b", [128, 8], F32)
    din("qc2b", [128, 8], F32)
    din("kc1b", [128, 2], F32)
    din("kc2b", [128, 2], F32)
    din("key_temp", [1, 1], F32)
    din("cos_t", [128, S], F32)
    din("sin_t", [128, S], F32)
    for nm, shp in [("identity", [128, 128]), ("ones_bf", [128, 128]),
                    ("swapP", [128, 128]), ("halfsum", [128, 64]),
                    ("e2sel", [16, 1024]), ("trimask", [128, 2048]),
                    ("e16", [128, 31])]:
        din(nm, shp, BF16)
    din("ones_f32", [128, 128], F32)

    io["out"] = nc.dram_tensor("out", [NOWN, D, CH], F32, kind="ExternalOutput").ap()
    if DBG_DUMP:
        io["dbg_x1"] = nc.dram_tensor("dbg_x1", [NSLAB, D, CW], BF16, kind="ExternalOutput").ap()
        io["dbg_q"] = nc.dram_tensor("dbg_q", [NOWN, LAT, CH], BF16, kind="ExternalOutput").ap()
        io["dbg_K"] = nc.dram_tensor("dbg_K", [KVD, S], BF16, kind="ExternalOutput").ap()
        io["dbg_V"] = nc.dram_tensor("dbg_V", [S, KVD], BF16, kind="ExternalOutput").ap()
        io["dbg_o"] = nc.dram_tensor("dbg_o", [NOWN, LAT, CH], BF16, kind="ExternalOutput").ap()

    sc = {}

    def dsc(name, shape, dtype, **kw):
        sc[name] = nc.dram_tensor(name, shape, dtype, kind="Internal", **kw).ap()

    dsc("wdqT", [D, D], BF16)
    dsc("wqkvT_bf", [D, QKV_C], BF16)
    dsc("woT_bf", [LAT, D], BF16)
    dsc("qc2T_bf", [3, LAT, LAT], BF16)
    dsc("kc2T_bf", [3, KVD, KVD], BF16)
    dsc("qc1T_bf", [3, 64, LAT], BF16)
    dsc("kc1T_bf", [3, 64, KVD], BF16)
    dsc("x1sp", [NSLAB, D, CW], BF16)
    dsc("x2sp", [NOWN, D, CH], BF16)
    dsc("qsp", [NOWN, LAT, CH], BF16)
    dsc("bounce", [4, 4], F32)
    dsc("Ksh", [KVD, S], BF16, addr_space="Shared")
    dsc("Vsh", [S, KVD], BF16, addr_space="Shared")
    dsc("flags", [1, 16], I32, addr_space="Shared")

    with contextlib.ExitStack() as est:
        E = est.enter_context
        sems = {nm: Sem(E(nc.semaphore(nm)))
                for nm in ["sx", "sw", "sv", "sq", "sm", "se", "sa", "sgp", "ssp", "sst", "sfl", "cv", "ca"]}
        pool = [Sem(E(nc.semaphore(f"p{i}"))) for i in range(60)]
        pool_idx = [0]
        stream_cache = {}

        def new_sems(k):
            r = pool[pool_idx[0]:pool_idx[0] + k]
            pool_idx[0] += k
            assert len(r) == k, "sem pool exhausted"
            return r

        def stream(name, k=2):
            if name not in stream_cache:
                stream_cache[name] = new_sems(k)
            return stream_cache[name]
        sems["_new"] = new_sems
        sems["_stream"] = stream
        cb = {}
        for nm, shape, dtype in [
            ("identity", [128, 128], BF16), ("ones_bf", [128, 128], BF16),
            ("ones_f32", [128, 128], F32), ("swapP", [128, 128], BF16),
            ("halfsum", [128, 64], BF16),
            ("e2sel", [16, 1024], BF16), ("trimask", [128, 2048], BF16),
            ("e16", [128, 31], BF16),
            ("qc1b", [128, 8], F32), ("qc2b", [128, 8], F32),
            ("kc1b", [128, 2], F32), ("kc2b", [128, 2], F32),
        ]:
            cb[nm] = E(nc.sbuf_tensor("cb_" + nm, shape, dtype))
        cb["negWF"] = E(nc.sbuf_tensor("negWF", [128, 32], F32))
        cb["negWQ"] = E(nc.sbuf_tensor("negWQ", [128, NCB], F32))
        cb["negC"] = E(nc.sbuf_tensor("negC", [128, 1], F32))
        cb["kt16"] = E(nc.sbuf_tensor("kt16", [16, 1], F32))
        cb["bdq"] = E(nc.sbuf_tensor("bdq", [128, 3, LAT], BF16))
        cb["bdk"] = E(nc.sbuf_tensor("bdk", [128, 3, KVD], BF16))

        emit_prolog(nc, io, sc, cb, sems)

        pid = nc.partition_id(ALL5)
        role = pid % 4
        regs = nc.alloc_registers("role", engines=ALL5)
        nc.regs_mov(regs, role)
        base = {k: v.n for k, v in sems.items() if not callable(v)}
        pool_base = pool_idx[0]
        pool_ns = [p.n for p in pool]
        for r in range(4):
            on_false = f"disp_{r+1}" if r < 3 else "join_all"
            nc.br_cmp(regs, r, on_true=f"vbody_{r}", on_false=on_false,
                      comp_op="IS_EQ", engines=ALL5)
            with nc.body(f"vbody_{r}", valid_engines=ALL5):
                for k, v in base.items():
                    sems[k].n = v
                for p, n0 in zip(pool, pool_ns):
                    p.n = n0
                emit_role(nc, io, sc, cb, sems, r)
                nc.br("join_all", engines=ALL5)
            if r < 3:
                nc.switch_bb(f"disp_{r+1}")
        nc.switch_bb("join_all")
    return nc



def emit_prolog(nc, io, sc, cb, sems):
    sgp, sq, sx, sm, se = sems["sgp"], sems["sq"], sems["sx"], sems["sm"], sems["se"]

    with nc.Block() as blk:
        @blk.gpsimd
        def _(g):
            sgp.inc(g.dma_start(out=sc["wqkvT_bf"][:, 0:1536], in_=io["w_qkvT"]))
            sgp.inc(g.dma_start(out=sc["woT_bf"], in_=io["w_oT"]))
            for j in range(3):
                sgp.inc(g.dma_start(out=sc["qc2T_bf"][j], in_=io["qc2T"][j]))
                sgp.inc(g.dma_start(out=sc["kc2T_bf"][j], in_=io["kc2T"][j]))
                sgp.inc(g.dma_start(out=sc["qc1T_bf"][j], in_=io["qc1T"][j]))
                sgp.inc(g.dma_start(out=sc["kc1T_bf"][j], in_=io["kc1T"][j]))
            g.wait_ge(sgp.h, sgp.n)

        @blk.sync
        def _(sy):
            for nm in ["identity", "ones_bf", "ones_f32", "swapP", "halfsum",
                       "e2sel", "trimask", "e16", "qc1b", "qc2b", "kc1b", "kc2b"]:
                sems["sw"].inc(sy.dma_start(out=cb[nm][:], in_=io[nm]))
            sy.wait_ge(sems["sw"].h, sems["sw"].n)

    dqp = sems["_stream"]("dqp")
    dqs = sems["_stream"]("dqs")
    with nc.Block() as blk, \
         nc.sbuf_tensor("dq_raw", [128, 2, D], BF16) as dq_raw, \
         nc.sbuf_tensor("dq_sc", [128, 2, D], F32) as dq_sc, \
         nc.sbuf_tensor("dq_out", [128, 2, D], BF16) as dq_out:
        se0, sq0 = se.n, sq.n

        @blk.gpsimd
        def _(g):
            for i in range(DT):
                if i >= 2:
                    g.wait_ge(se.h, se0 + i - 1)
                g.dma_start(out=dq_raw[:, i % 2], in_=io["w_fp8T"][128 * i:128 * (i + 1), :]).then_inc(dqp[i % 2].h, 16)
                g.dma_start(out=dq_sc[:, i % 2],
                            in_=io["scale_exp"][i:i + 1, :].partition_broadcast(128)).then_inc(dqp[i % 2].h, 16)
        dqp[0].n += 32 * (DT // 2)
        dqp[1].n += 32 * (DT // 2)

        @blk.vector
        def _(v):
            for i in range(DT):
                v.wait_ge(dqp[i % 2].h, 32 * (i // 2 + 1))
                if i >= 2:
                    v.wait_ge(dqs[i % 2].h, 16 * ((i - 2) // 2 + 1))
                v.tensor_tensor(dq_out[:, i % 2], dq_raw[:, i % 2], dq_sc[:, i % 2], AL.mult).then_inc(se.h, 1)
        se.n += DT

        @blk.scalar
        def _(s):
            for i in range(DT):
                s.wait_ge(se.h, se0 + i + 1)
                s.dma_start(out=sc["wdqT"][128 * i:128 * (i + 1), :], in_=dq_out[:, i % 2]).then_inc(dqs[i % 2].h, 16)
            for par in range(2):
                s.wait_ge(dqs[par].h, 16 * (DT // 2))
        dqs[0].n += 16 * (DT // 2)
        dqs[1].n += 16 * (DT // 2)

    with nc.Block() as blk, \
         nc.sbuf_tensor("qm_in", [128, 2, LAT], BF16) as qm_in, \
         nc.sbuf_tensor("qm_acc", [128, 2, 64], F32) as qm_acc, \
         nc.sbuf_tensor("qm_out", [128, 2, 64], BF16) as qm_out:
        sx0, se0, sq0 = sx.n, se.n, sq.n

        qmp = sems["_new"](2)
        qms = sems["_new"](2)

        @blk.sync
        def _(sy):
            for i in range(DT):
                if i >= 2:
                    sy.wait_ge(se.h, se0 + (i - 1))
                sy.dma_start(out=qm_in[:, i % 2], in_=sc["wqkvT_bf"][128 * i:128 * (i + 1), 0:LAT]).then_inc(qmp[i % 2].h, 16)
        qmp[0].n += 16 * (DT // 2)
        qmp[1].n += 16 * (DT // 2)

        @blk.vector
        def _(v):
            for i in range(DT):
                v.wait_ge(qmp[i % 2].h, 16 * (i // 2 + 1))
                if i >= 2:
                    v.wait_ge(qms[i % 2].h, 16 * ((i - 2) // 2 + 1))
                p = i % 2
                hv = qm_in[:, p].rearrange("p (h d) -> p d h", d=64)
                chain(v, sems["cv"], [
                    lambda: v.tensor_reduce(qm_acc[:, p], hv, mybir.AxisListType.X, AL.add),
                    lambda: v.tensor_scalar_mul(qm_out[:, p], qm_acc[:, p], 1.0 / 32.0),
                ]).then_inc(se.h, 1)
        se.n += DT

        @blk.scalar
        def _(s):
            for i in range(DT):
                s.wait_ge(se.h, se0 + i + 1)
                s.dma_start(out=sc["wqkvT_bf"][128 * i:128 * (i + 1), 1536:1600],
                            in_=qm_out[:, i % 2]).then_inc(qms[i % 2].h, 16)
            for par in range(2):
                s.wait_ge(qms[par].h, 16 * (DT // 2))
        qms[0].n += 16 * (DT // 2)
        qms[1].n += 16 * (DT // 2)

    with nc.Block() as blk:
        sfl0 = sems["sfl"].n
        sq0 = sems["sq"].n

        @blk.gpsimd
        def _(g):
            g.memset(cb["bdq"][:], 0.0)
            g.memset(cb["bdk"][:], 0.0)
            g.engine_nop().then_inc(sems["sfl"].h, 1)
        sems["sfl"].n += 1

        @blk.scalar
        def _(s):
            s.wait_ge(sems["sfl"].h, sfl0 + 1)
            for j in range(3):
                for (dst, src) in [(cb["bdq"], sc["qc1T_bf"]), (cb["bdk"], sc["kc1T_bf"])]:
                    s.dma_start(
                        out=dst[0:64, j].rearrange("p (m c) -> p m c", c=128)[:, :, 0:64],
                        in_=src[j].rearrange("p (m c) -> p m c", c=128)[:, :, 0:64]).then_inc(sems["sq"].h, 16)
                    s.dma_start(
                        out=dst[64:128, j].rearrange("p (m c) -> p m c", c=128)[:, :, 64:128],
                        in_=src[j].rearrange("p (m c) -> p m c", c=128)[:, :, 64:128]).then_inc(sems["sq"].h, 16)
            s.wait_ge(sems["sq"].h, sq0 + 16 * 12)
        sems["sq"].n += 16 * 12

    with nc.Block() as blk, \
         nc.sbuf_tensor("ws_w", [128, 2, D], BF16) as ws_w, \
         nc.sbuf_tensor("ws_q", [128, 2, QKV_C], BF16) as ws_q, \
         nc.sbuf_tensor("ws_tmp", [1, 4], F32) as ws_tmp, \
         nc.psum_tensor("ps_ws", [128, 512], F32) as ps_ws, \
         nc.psum_tensor("ps_wq", [128, 512], F32) as ps_wq:
        sx0, sm0, se0, sq0, sgp0 = sx.n, sm.n, se.n, sq.n, sgp.n

        wsp = sems["_new"](2)
        wqp = sems["_new"](2)

        @blk.sync
        def _(sy):
            for i in range(DT):
                if i >= 2:
                    sy.wait_ge(sm.h, sm0 + 32 * (i - 1))
                sy.dma_start(out=ws_w[:, i % 2], in_=sc["wdqT"][128 * i:128 * (i + 1), :]).then_inc(wsp[i % 2].h, 16)
            for i in range(DT):
                if i >= 2:
                    sy.wait_ge(sm.h, sm0 + 32 * DT + NCB * (i - 1))
                sy.dma_start(out=ws_q[:, i % 2], in_=sc["wqkvT_bf"][128 * i:128 * (i + 1), :]).then_inc(wqp[i % 2].h, 16)
        for p in (wsp, wqp):
            p[0].n += 16 * (DT // 2)
            p[1].n += 16 * (DT // 2)

        @blk.tensor
        def _(t):
            for i in range(DT):
                t.wait_ge(wsp[i % 2].h, 16 * (i // 2 + 1))
                for ob in range(32):
                    t.matmul(ps_ws[:, ob:ob + 1], ws_w[:, i % 2, 128 * ob:128 * (ob + 1)],
                             cb["ones_bf"][:, 0:1], start=(i == 0 and ob == 0),
                             stop=(i == DT - 1), skip_group_check=True).then_inc(sm.h, 1)
            for i in range(DT):
                t.wait_ge(wqp[i % 2].h, 16 * (i // 2 + 1))
                for cbk in range(NCB):
                    w = 64 if cbk == 12 else 128
                    t.matmul(ps_wq[0:w, cbk:cbk + 1], ws_q[:, i % 2, 128 * cbk:128 * cbk + w],
                             cb["ones_bf"][:, 0:1], start=(i == 0 and cbk == 0),
                             stop=(i == DT - 1), skip_group_check=True).then_inc(sm.h, 1)
        sm.n += (32 + NCB) * DT

        @blk.vector
        def _(v):
            v.wait_ge(sm.h, sm0 + 32 * DT)
            v.tensor_scalar_mul(cb["negWF"][:], ps_ws[:, 0:32], -1.0).then_inc(se.h, 1)
            v.wait_ge(sm.h, sm0 + (32 + NCB) * DT)
            v.tensor_scalar_mul(cb["negWQ"][:, 0:NCB - 1], ps_wq[:, 0:NCB - 1], -1.0)
            v.tensor_scalar_mul(cb["negWQ"][0:64, NCB - 1:NCB], ps_wq[0:64, NCB - 1:NCB], -1.0).then_inc(se.h, 1)
        se.n += 2

        @blk.gpsimd
        def _(g):
            g.dma_start(out=ws_tmp[0:1, 0:1], in_=io["key_temp"]).then_inc(sems["sgp"].h, 16)
        sems["sgp"].n += 16

        @blk.scalar
        def _(s):
            s.wait_ge(sems["sgp"].h, sgp0 + 16)
            chain(s, sems["ca"], [
                lambda: s.activation(ws_tmp[0:1, 2:3], ws_tmp[0:1, 0:1], AF.Square),
                lambda: s.activation(ws_tmp[0:1, 0:1], ws_tmp[0:1, 2:3], AF.Sqrt),
            ]).then_inc(sems["sa"].h, 1)
        sems["sa"].n += 1

        @blk.vector
        def _(v):
            v.wait_ge(sems["sa"].h, sems["sa"].n)
            v.tensor_scalar_mul(ws_tmp[0:1, 1:2], ws_tmp[0:1, 0:1], -0.126).then_inc(se.h, 1)
        se.n += 1

        @blk.scalar
        def _(s):
            s.wait_ge(se.h, se0 + 3)
            s.dma_start(out=sc["bounce"][0:1, 0:2], in_=ws_tmp[0:1, 0:2]).then_inc(sq.h, 16)
        sq.n += 16

        @blk.sync
        def _(sy):
            sy.wait_ge(sq.h, sq0 + 16)
            sy.dma_start(out=cb["negC"][:], in_=sc["bounce"][0:1, 1:2].partition_broadcast(128)).then_inc(sx.h, 16)
            sy.dma_start(out=cb["kt16"][:], in_=sc["bounce"][0:1, 0:1].partition_broadcast(16)).then_inc(sx.h, 16)
            sy.wait_ge(sx.h, sx.n + 32)
        sx.n += 32



def emit_stats2(nc, cb, sems, blk, xg, ab, mb, gate, fold, W):
    ng = len(xg)
    NT = 2 if W == CW else 1
    CL = 264 if NT == 2 else 512
    sm, se, sa, sv = sems["sm"], sems["se"], sems["sa"], sems["sv"]
    sa0, sm0, se0, sv0 = sa.n, sm.n, se.n, sv.n

    def gpp(ap):
        v = ap.rearrange("p (g c) -> p g c", g=2)
        return v[:, :, 0:CL] if NT == 2 else v[:, 0:1, 0:CL]

    def hpw(ap):
        return ap.rearrange("p (g c) -> p g c", g=NT)

    ctx = []

    def A(t):
        ctx.append(t)
        return t.__enter__()

    xsq = A(nc.sbuf_tensor(f"stq{_uid()}", [128, 2, W], BF16))
    vst = A(nc.sbuf_tensor(f"stv{_uid()}", [1, ng, 6, W], F32))
    vbf = A(nc.sbuf_tensor(f"stb{_uid()}", [1, ng, 2, W], BF16))
    ps1 = A(nc.psum_tensor(f"stp1_{_uid()}", [1, 1024], F32))
    ps2 = A(nc.psum_tensor(f"stp2_{_uid()}", [1, 1024], F32))
    psb = A(nc.psum_tensor(f"stpb_{_uid()}", [128, 2048], F32))

    MU, M2, VAR, LNV, AA, AF_ = range(6)

    def vf(g, f):
        return hpw(vst[0:1, g, f])

    @blk.scalar
    def _(s):
        for g in range(ng):
            for i in range(DT):
                gs, gv = gate(g, i)
                s.wait_ge(gs.h, gv)
                if g * DT + i >= 2:
                    s.wait_ge(sm.h, sm0 + 2 * NT * (g * DT + i - 1))
                s.activation(xsq[:, i % 2], xg[g][:, i], AF.Square).then_inc(sa.h, 1)
        for g in range(ng):
            s.wait_ge(se.h, se0 + g + 1)
            chain(s, sems["ca"], [
                lambda: s.activation(vf(g, LNV), vf(g, VAR), AF.Ln),
                lambda: s.activation(vf(g, AA), vf(g, LNV), AF.Exp, scale=-0.5),
            ]).then_inc(sa.h, 1)
    sa.n += ng * DT + ng

    @blk.tensor
    def _(t):
        for g in range(ng):
            if g >= 1:
                t.wait_ge(se.h, se0 + g)
            for i in range(DT):
                t.wait_ge(sa.h, sa0 + g * DT + i + 1)
                for th in range(NT):
                    t.matmul(ps1[0:1, 512 * th:512 * th + CL],
                             cb["ones_bf"][:, 0:1], xg[g][:, i, CL * th:CL * th + CL],
                             start=(i == 0), stop=(i == DT - 1)).then_inc(sm.h, 1)
                    t.matmul(ps2[0:1, 512 * th:512 * th + CL],
                             cb["ones_bf"][:, 0:1], xsq[:, i % 2, CL * th:CL * th + CL],
                             start=(i == 0), stop=(i == DT - 1)).then_inc(sm.h, 1)
        for g in range(ng):
            t.wait_ge(sv.h, sv0 + g + 1)
            for th in range(NT):
                t.matmul(psb[:, 512 * th:512 * th + CL], cb["ones_bf"][0:1, :],
                         vbf[0:1, g, 0, CL * th:CL * th + CL],
                         start=True, stop=True).then_inc(sm.h, 1)
                t.matmul(psb[:, 1024 + 512 * th:1024 + 512 * th + CL], cb["ones_bf"][0:1, :],
                         vbf[0:1, g, 1, CL * th:CL * th + CL],
                         start=True, stop=True).then_inc(sm.h, 1)
    sm.n += ng * 2 * NT * DT + ng * 2 * NT

    @blk.vector
    def _(v):
        for g in range(ng):
            v.wait_ge(sm.h, sm0 + (g + 1) * 2 * NT * DT)
            mu, m2, var = vf(g, MU), vf(g, M2), vf(g, VAR)
            chain(v, sems["cv"], [
                lambda: v.tensor_scalar_mul(mu, gpp(ps1[0:1, :]), 1.0 / D),
                lambda: v.tensor_scalar_mul(m2, gpp(ps2[0:1, :]), 1.0 / D),
                lambda: v.tensor_tensor(var, mu, mu, AL.mult),
                lambda: v.tensor_sub(var, m2, var),
                lambda: v.tensor_scalar_add(var, var, EPS),
            ]).then_inc(se.h, 1)
        for g in range(ng):
            v.wait_ge(sa.h, sa0 + ng * DT + g + 1)
            aa, mu, af = vf(g, AA), vf(g, MU), vf(g, AF_)
            chain(v, sems["cv"], [
                lambda: v.tensor_scalar_mul(af, aa, fold),
                lambda: v.tensor_copy(hpw(vbf[0:1, g, 0]), af),
                lambda: v.tensor_tensor(af, af, mu, AL.mult),
                lambda: v.tensor_copy(hpw(vbf[0:1, g, 1]), af),
            ]).then_inc(sv.h, 1)
            v.wait_ge(sm.h, sm0 + ng * 2 * NT * DT + (g + 1) * 2 * NT)
            v.tensor_copy(hpw(ab[g][:]), gpp(psb[:, 0:1024]) if NT == 1 else g2(psb[:, 0:1024]))
            v.tensor_copy(hpw(mb[g][:]), gpp(psb[:, 1024:2048]) if NT == 1 else g2(psb[:, 1024:2048])).then_inc(se.h, 1)
    se.n += 2 * ng
    sv.n += ng

    ab_ready = [se0 + ng + g + 1 for g in range(ng)]
    for t in reversed(ctx):
        t.__exit__(None, None, None)
    return ab_ready, se0 + 2 * ng


_UID = [0]


def _uid():
    _UID[0] += 1
    return _UID[0]


def emit_ffn_group(nc, io, sc, cb, sems, grp_slabs, mode, own_idx=None):
    ng = len(grp_slabs)
    W = CW if mode == "ffn1" else CH
    NT = 2 if mode == "ffn1" else 1
    CL = 264 if NT == 2 else 512
    sx, sm, se, sa, sq, sst, sw = (sems[k] for k in ["sx", "sm", "se", "sa", "sq", "sst", "sw"])

    with contextlib.ExitStack() as est, nc.Block() as blk:
        E = est.enter_context
        xg = [E(nc.sbuf_tensor(f"fx{_uid()}", [128, DT, W], BF16)) for g in range(ng)]
        ab = [E(nc.sbuf_tensor(f"fab{_uid()}", [128, W], BF16)) for g in range(ng)]
        mb = [E(nc.sbuf_tensor(f"fmb{_uid()}", [128, W], BF16)) for g in range(ng)]
        wcol = E(nc.sbuf_tensor(f"fw{_uid()}", [128, 2, D], BF16))
        ev1 = E(nc.sbuf_tensor(f"fe1{_uid()}", [128, 2, W], F32))
        ev2 = E(nc.sbuf_tensor(f"fe2{_uid()}", [128, 2, W], F32))
        ost = E(nc.sbuf_tensor(f"fo{_uid()}", [128, 2, W], F32 if mode == "ffn2" else BF16))
        sx0, se0 = sx.n, se.n

        if mode == "ffn1":
            xstg = E(nc.sbuf_tensor(f"fxs{_uid()}", [128, 2, W], F32))
            xlp = sems["_stream"]("xlp")
            xlp0 = [xlp[0].n, xlp[1].n]

            @blk.sync
            def _(sy):
                for g in range(ng):
                    for i in range(DT):
                        k = g * DT + i
                        if k >= 2:
                            sy.wait_ge(se.h, se0 + k - 1)
                        sy.dma_start(out=xstg[:, k % 2],
                                     in_=io["x_sl"][grp_slabs[g], 128 * i:128 * (i + 1), :]).then_inc(xlp[k % 2].h, 16)
            for par in range(2):
                xlp[par].n += 16 * ((ng * DT + 1 - par) // 2)

            @blk.vector
            def _(v):
                for g in range(ng):
                    for i in range(DT):
                        k = g * DT + i
                        v.wait_ge(xlp[k % 2].h, xlp0[k % 2] + 16 * (k // 2 + 1))
                        v.tensor_copy(xg[g][:, i], xstg[:, k % 2]).then_inc(se.h, 1)
            se.n += ng * DT
            gate = lambda g, i: (se, se0 + g * DT + i + 1)
        else:
            @blk.sync
            def _(sy):
                for g in range(ng):
                    for i in range(DT):
                        sy.dma_start(out=xg[g][:, i],
                                     in_=sc["x2sp"][grp_slabs[g], 128 * i:128 * (i + 1), :]).then_inc(sx.h, 16)
                sy.wait_ge(sx.h, sx0 + 16 * ng * DT)
            sx.n += 16 * ng * DT
            gate = lambda g, i: (sx, sx0 + 16 * ng * DT)

        ab_ready, stats_done = emit_stats2(nc, cb, sems, blk, xg, ab, mb, gate, 0.5, W)

        sm1, sq0, sst0, sev0 = sems["sm"].n, sq.n, sst.n, sems["se"].n
        ps_f = [E(nc.psum_tensor(f"fpf{_uid()}", [128, 1024], F32)) for i in range(2)]
        NOB = 32
        wlp = sems["_stream"]("wlp")
        wlp0 = [wlp[0].n, wlp[1].n]
        stp = sems["_stream"]("stp")
        stp0 = [stp[0].n, stp[1].n]

        @blk.sync
        def _(sy):
            for ob in range(NOB):
                if ob >= 2:
                    sy.wait_ge(sems["sm"].h, sm1 + (ob - 1) * ng)
                for part in range(4):
                    sy.dma_start(
                        out=wcol[:, ob % 2, 1024 * part:1024 * (part + 1)]
                            .rearrange("p (it o) -> p it o", o=128),
                        in_=sc["wdqT"].rearrange("(it p) o -> p it o", p=128)
                            [:, 8 * part:8 * (part + 1), 128 * ob:128 * (ob + 1)]).then_inc(wlp[ob % 2].h, 16)
        wlp[0].n += 64 * (NOB // 2)
        wlp[1].n += 64 * (NOB // 2)

        @blk.tensor
        def _(t):
            t.wait_ge(sems["se"].h, stats_done)
            for ob in range(NOB):
                t.wait_ge(wlp[ob % 2].h, wlp0[ob % 2] + 64 * (ob // 2 + 1))
                for g in range(ng):
                    u = ob * ng + g
                    if u >= 2:
                        t.wait_ge(sems["se"].h, sev0 + u - 1)
                    for i in range(DT):
                        for th in range(NT):
                            mm = t.matmul(ps_f[u % 2][:, 512 * th:512 * th + CL],
                                          wcol[:, ob % 2, 128 * i:128 * (i + 1)],
                                          xg[g][:, i, CL * th:CL * th + CL],
                                          start=(i == 0), stop=(i == DT - 1))
                    mm.then_inc(sems["sm"].h, 1)
        sems["sm"].n += NOB * ng

        @blk.vector
        def _(v):
            ecnt = 0
            for ob in range(NOB):
                for g in range(ng):
                    v.wait_ge(sems["sm"].h, sm1 + ob * ng + g + 1)
                    if ecnt >= 2:
                        v.wait_ge(stp[ecnt % 2].h, stp0[ecnt % 2] + 16 * ((ecnt - 2) // 2 + 1))
                    pe = ecnt % 2
                    p = ps_f[(ob * ng + g) % 2]
                    psv = p.rearrange("p (g c) -> p g c", g=2)[:, 0:NT, 0:CL]
                    abv = ab[g][:].rearrange("p (g c) -> p g c", g=NT)
                    mbv = mb[g][:].rearrange("p (g c) -> p g c", g=NT)
                    e1 = ev1[:, pe].rearrange("p (g c) -> p g c", g=NT)
                    e2v = ev2[:, pe].rearrange("p (g c) -> p g c", g=NT)
                    ov = ost[:, pe].rearrange("p (g c) -> p g c", g=NT)
                    chain(v, sems["cv"], [
                        lambda: v.tensor_tensor(e1, psv, abv, AL.mult),
                        lambda: v.scalar_tensor_tensor(e2v, mbv, cb["negWF"][:, ob:ob + 1], e1, AL.mult, AL.add),
                        lambda: v.tensor_tensor(ov, xg[g][:, ob].rearrange("p (g c) -> p g c", g=NT), e2v, AL.add),
                    ]).then_inc(sems["se"].h, 1)
                    ecnt += 1
        sems["se"].n += NOB * ng

        @blk.scalar
        def _(s):
            ecnt = 0
            for ob in range(NOB):
                for g in range(ng):
                    s.wait_ge(sems["se"].h, sev0 + ecnt + 1)
                    pe = ecnt % 2
                    if mode == "ffn1":
                        s.dma_start(out=sc["x1sp"][grp_slabs[g], 128 * ob:128 * (ob + 1), :],
                                    in_=ost[:, pe]).then_inc(stp[pe].h, 16)
                    else:
                        s.dma_start(out=io["out"][own_idx[g], 128 * ob:128 * (ob + 1), :],
                                    in_=ost[:, pe]).then_inc(stp[pe].h, 16)
                    ecnt += 1
            for par in range(2):
                s.wait_ge(stp[par].h, stp0[par] + 16 * ((NOB * ng + 1 - par) // 2))
        for par in range(2):
            stp[par].n += 16 * ((NOB * ng + 1 - par) // 2)




def emit_qkv_group(nc, io, sc, cb, sems, est, grp_slabs, is_own):
    ng = len(grp_slabs)
    W = CW
    CL = 264
    sx = sems["sx"]
    cbs = list(range(NCB)) if is_own else list(range(8, NCB))

    qv = [est.enter_context(nc.sbuf_tensor(f"qv{_uid()}", [128, NCB, W], BF16)) for _ in range(ng)]

    with contextlib.ExitStack() as e1, nc.Block() as blk:
        E = e1.enter_context
        x1g = [E(nc.sbuf_tensor(f"bx{_uid()}", [128, DT, W], BF16)) for g in range(ng)]
        ab = [E(nc.sbuf_tensor(f"bab{_uid()}", [128, W], BF16)) for g in range(ng)]
        mb = [E(nc.sbuf_tensor(f"bmb{_uid()}", [128, W], BF16)) for g in range(ng)]
        wcol = E(nc.sbuf_tensor(f"bw{_uid()}", [128, 2, D], BF16))
        ev1 = E(nc.sbuf_tensor(f"be1{_uid()}", [128, 2, W], F32))
        sx0 = sx.n

        @blk.sync
        def _(sy):
            for g in range(ng):
                for i in range(DT):
                    sy.dma_start(out=x1g[g][:, i],
                                 in_=sc["x1sp"][grp_slabs[g], 128 * i:128 * (i + 1), :]).then_inc(sx.h, 16)
            sy.wait_ge(sx.h, sx0 + 16 * ng * DT)
        sx.n += 16 * ng * DT
        gate = lambda g, i: (sx, sx0 + 16 * ng * DT)

        ab_ready, stats_done = emit_stats2(nc, cb, sems, blk, x1g, ab, mb, gate, 1.0, W)

        sm1, sev0 = sems["sm"].n, sems["se"].n
        wlp = sems["_stream"]("wlp")
        wlp0 = [wlp[0].n, wlp[1].n]
        ps_q = [E(nc.psum_tensor(f"bpq{_uid()}", [128, 1024], F32)) for i in range(2)]
        NC = len(cbs)

        @blk.sync
        def _(sy):
            for ci, cbk in enumerate(cbs):
                if ci >= 2:
                    sy.wait_ge(sems["sm"].h, sm1 + (ci - 1) * ng)
                wd = 64 if cbk == 12 else 128
                for part in range(4):
                    sy.dma_start(
                        out=wcol[:, ci % 2, wd * 8 * part:wd * 8 * (part + 1)]
                            .rearrange("p (it o) -> p it o", o=wd),
                        in_=sc["wqkvT_bf"].rearrange("(it p) o -> p it o", p=128)
                            [:, 8 * part:8 * (part + 1), 128 * cbk:128 * cbk + wd]).then_inc(wlp[ci % 2].h, 16)
        for par in range(2):
            wlp[par].n += 64 * ((NC + 1 - par) // 2)

        @blk.tensor
        def _(t):
            t.wait_ge(sems["se"].h, stats_done)
            for ci, cbk in enumerate(cbs):
                t.wait_ge(wlp[ci % 2].h, wlp0[ci % 2] + 64 * (ci // 2 + 1))
                wd = 64 if cbk == 12 else 128
                for g in range(ng):
                    u = ci * ng + g
                    if u >= 2:
                        t.wait_ge(sems["se"].h, sev0 + u - 1)
                    for i in range(DT):
                        for th in range(2):
                            mm = t.matmul(ps_q[u % 2][0:wd, 512 * th:512 * th + CL],
                                          wcol[:, ci % 2, wd * i:wd * (i + 1)],
                                          x1g[g][:, i, CL * th:CL * th + CL],
                                          start=(i == 0), stop=(i == DT - 1))
                    mm.then_inc(sems["sm"].h, 1)
        sems["sm"].n += NC * ng

        @blk.vector
        def _(v):
            for ci, cbk in enumerate(cbs):
                wd = 64 if cbk == 12 else 128
                for g in range(ng):
                    u = ci * ng + g
                    v.wait_ge(sems["sm"].h, sm1 + u + 1)
                    p = ps_q[u % 2]
                    psv = p[0:wd].rearrange("p (g c) -> p g c", g=2)[:, :, 0:CL]
                    abv = ab[g][0:wd].rearrange("p (g c) -> p g c", g=2)
                    mbv = mb[g][0:wd].rearrange("p (g c) -> p g c", g=2)
                    e1v = ev1[0:wd, u % 2].rearrange("p (g c) -> p g c", g=2)
                    ov = qv[g][0:wd, cbk].rearrange("p (g c) -> p g c", g=2)
                    chain(v, sems["cv"], [
                        lambda: v.tensor_tensor(e1v, psv, abv, AL.mult),
                        lambda: v.scalar_tensor_tensor(ov, mbv, cb["negWQ"][0:wd, cbk:cbk + 1],
                                                       e1v, AL.mult, AL.add),
                    ]).then_inc(sems["se"].h, 1)
        sems["se"].n += NC * ng
    return qv


def emit_conv(nc, cb, sems, *, bd, w2, nbd, ni, src, src_off, src_col0, dst, dstW,
              dstcol0, bias, ps, zero_cols=None):
    sm, sa, se = sems["sm"], sems["sa"], sems["se"]
    nwid = dstW - dstcol0
    half = nwid // 2
    off = src_col0 - dstcol0
    with nc.Block() as blk:
        sm0, sa0, se0 = sm.n, sa.n, se.n

        @blk.tensor
        def _(t):
            for m in range(nbd):
                if m >= 2:
                    t.wait_ge(se.h, se0 + m - 1)
                for th in range(2):
                    o0 = dstcol0 + half * th
                    first = True
                    for i in range(1 if bd is not None else ni):
                        for j in range(3):
                            mm = t.matmul(ps[m % 2][:, 512 * th:512 * th + half],
                                          (bd[:, j, 128 * m:128 * (m + 1)] if bd is not None
                                           else w2[:, j, i, 128 * m:128 * (m + 1)]),
                                          src[:, (src_off + m if bd is not None else i),
                                              o0 + off + j - 2:o0 + off + j - 2 + half],
                                          start=first,
                                          stop=(i == (0 if bd is not None else ni - 1) and j == 2))
                            first = False
                mm.then_inc(sm.h, 1)
        sm.n += nbd

        @blk.scalar
        def _(s):
            for m in range(nbd):
                s.wait_ge(sm.h, sm0 + m + 1)
                psv = ps[m % 2].rearrange("p (g c) -> p g c", g=2)[:, :, 0:half]
                ov = dst[:, m, dstcol0:].rearrange("p (g c) -> p g c", g=2)
                s.activation(ov, psv, AF.Identity, bias=bias[:, m:m + 1]).then_inc(sa.h, 1)
        sa.n += nbd

        @blk.vector
        def _(v):
            for m in range(nbd):
                v.wait_ge(sa.h, sa0 + m + 1)
                v.engine_nop().then_inc(se.h, 1)
        se.n += nbd

        if zero_cols is not None:
            @blk.gpsimd
            def _(g):
                g.wait_ge(sa.h, sa0 + nbd)
                g.memset(dst[:, 0:nbd, zero_cols[0]:zero_cols[1]], 0.0)


def emit_norm_rope(nc, io, cb, sems, *, src, ntile, nheads, chunk, kt, dst, sq_t, rsb,
                   rbf, rbt, cosb, ps_s, ps_r):
    sm, sa, se = sems["sm"], sems["sa"], sems["se"]
    with nc.Block() as blk:
        sm0, sa0, se0 = sm.n, sa.n, se.n

        @blk.scalar
        def _(s):
            for m in range(ntile):
                s.activation(sq_t[:, m], src[:, m], AF.Square).then_inc(sa.h, 1)
        sa.n += ntile

        @blk.tensor
        def _(t):
            for m in range(ntile):
                t.wait_ge(sa.h, sa0 + m + 1)
                mm = t.matmul(ps_s[0:nheads, 0:CH],
                              cb["e16"][:, 15 - 2 * m:15 - 2 * m + nheads],
                              sq_t[:, m],
                              start=(m == 0), stop=(m == ntile - 1))
                mm.then_inc(sm.h, 1)
        sm.n += ntile

        @blk.vector
        def _(v):
            v.wait_ge(sm.h, sm0 + ntile)
            v.tensor_scalar_max(rsb[0:nheads, :], ps_s[0:nheads, 0:CH], 1e-24).then_inc(se.h, 1)
        se.n += 1

        @blk.scalar
        def _(s):
            s.wait_ge(se.h, se0 + 1)
            chain(s, sems["ca"], [
                lambda: s.activation(rsb[0:nheads, :], rsb[0:nheads, :], AF.Ln),
                lambda: s.activation(rsb[0:nheads, :], rsb[0:nheads, :], AF.Exp, scale=-0.5),
            ]).then_inc(sa.h, 1)
        sa.n += 1

        @blk.vector
        def _(v):
            v.wait_ge(sa.h, sa0 + ntile + 1)
            ops = []
            if kt:
                ops.append(lambda: v.tensor_scalar(rsb[0:nheads, :], rsb[0:nheads, :],
                                                   cb["kt16"][0:nheads, 0:1], None, AL.mult))
            ops.append(lambda: v.tensor_copy(rbf[0:nheads, :], rsb[0:nheads, :]))
            chain(v, sems["cv"], ops).then_inc(se.h, 1)
        se.n += 1

    for m in range(ntile):
        with nc.Block() as blk:
            sm0, sa0, se0 = sm.n, sa.n, se.n

            @blk.tensor
            def _(t):
                t.matmul(ps_r[:, 0:CH], cb["e2sel"][0:nheads, 128 * m:128 * (m + 1)],
                         rbf[0:nheads, :], start=True, stop=True).then_inc(sm.h, 1)
            sm.n += 1

            @blk.vector
            def _(v):
                v.wait_ge(sm.h, sm0 + 1)
                chain(v, sems["cv"], [
                    lambda: v.tensor_copy(rbt[:], ps_r[:, 0:CH]),
                    lambda: v.tensor_tensor(sq_t[:, m], src[:, m], rbt[:], AL.mult),
                ]).then_inc(se.h, 1)
            se.n += 1

            @blk.tensor
            def _(t):
                t.wait_ge(sems["se"].h, se0 + 1)
                t.matmul(ps_r[:, 0:CH], cb["swapP"][:], sq_t[:, m],
                         start=True, stop=True).then_inc(sm.h, 1)
            sm.n += 1

            @blk.vector
            def _(v):
                v.wait_ge(sm.h, sm0 + 2)
                e1 = rbt[:]
                chain(v, sems["cv"], [
                    lambda: v.tensor_tensor(e1, sq_t[:, m], cosb[:, 0], AL.mult),
                    lambda: v.tensor_tensor(dst[:, m], ps_r[:, 0:CH], cosb[:, 1], AL.mult),
                    lambda: v.tensor_tensor(dst[:, m], dst[:, m], e1, AL.add),
                ]).then_inc(se.h, 1)
            se.n += 1


def emit_convkv_slab(nc, io, sc, cb, sems, slab, chunk, is_own, qv, own_idx):
    sx, sm, se, sa, sq, sw = (sems[k] for k in ["sx", "sm", "se", "sa", "sq", "sw"])
    zero_hist = (chunk == 0)

    with contextlib.ExitStack() as est:
        E = est.enter_context
        c1 = E(nc.sbuf_tensor(f"c1_{_uid()}", [128, 8, CW], BF16))
        c2 = E(nc.sbuf_tensor(f"c2_{_uid()}", [128, 8, CH], BF16))
        sq_t = E(nc.sbuf_tensor(f"sqt{_uid()}", [128, 8, CH], BF16))
        qr = E(nc.sbuf_tensor(f"qr_{_uid()}", [128, 8, CH], BF16))
        rep = E(nc.sbuf_tensor(f"rep{_uid()}", [128, CH], BF16))
        rsb = E(nc.sbuf_tensor(f"rs_{_uid()}", [16, CH], F32))
        rbf = E(nc.sbuf_tensor(f"rb_{_uid()}", [16, CH], BF16))
        rbt = E(nc.sbuf_tensor(f"rt_{_uid()}", [128, CH], BF16))
        cosb = E(nc.sbuf_tensor(f"cs_{_uid()}", [128, 2, CH], F32))
        vstage = E(nc.sbuf_tensor(f"vs_{_uid()}", [128, 4, 256], BF16))
        kw2 = E(nc.sbuf_tensor(f"kw_{_uid()}", [128, 3, 2, 256], BF16))
        ps_a = [E(nc.psum_tensor(f"pa_{_uid()}", [128, 1024], F32)) for _ in range(2)]
        ps_s = E(nc.psum_tensor(f"pss{_uid()}", [128, 512], F32))
        ps_r = E(nc.psum_tensor(f"psr{_uid()}", [128, 512], F32))
        ps_t = E(nc.psum_tensor(f"pst{_uid()}", [128, 128], BF16))

        with nc.Block() as blk:
            sx0 = sx.n

            @blk.sync
            def _(sy):
                sy.dma_start(out=cosb[:, 0], in_=io["cos_t"][:, chunk * CH:(chunk + 1) * CH]).then_inc(sx.h, 16)
                sy.dma_start(out=cosb[:, 1], in_=io["sin_t"][:, chunk * CH:(chunk + 1) * CH]).then_inc(sx.h, 16)
                for j in range(3):
                    sy.dma_start(out=kw2[:, j],
                                 in_=sc["kc2T_bf"][j].rearrange("(it p) o -> p it o", p=128)).then_inc(sx.h, 16)
                sy.wait_ge(sx.h, sx0 + 16 * 5)
            sx.n += 16 * 5

        emit_conv(nc, cb, sems, bd=cb["bdk"], w2=None, nbd=2, ni=2, src=qv, src_off=8,
                  src_col0=14, dst=c1, dstW=CW, dstcol0=14, bias=cb["kc1b"], ps=ps_a,
                  zero_cols=(14, 16) if zero_hist else None)
        if DBG_CONVKV < 2:
            return

        emit_conv(nc, cb, sems, bd=None, w2=kw2, nbd=2, ni=2, src=c1, src_off=0,
                  src_col0=HALO, dst=c2, dstW=CH, dstcol0=0, bias=cb["kc2b"], ps=ps_a,
                  zero_cols=None)
        if DBG_CONVKV < 3:
            return

        with nc.Block() as blk:
            se0, sq0 = se.n, sq.n

            @blk.vector
            def _(v):
                v.tensor_copy(rep[0:64, :], qv[0:64, 12, HALO:]).then_inc(se.h, 1)
            se.n += 1

            @blk.scalar
            def _(s):
                s.wait_ge(se.h, se0 + 1)
                s.dma_start(out=rep[64:128, :], in_=rep[0:64, :]).then_inc(sq.h, 16)
            sq.n += 16

            @blk.vector
            def _(v):
                v.wait_ge(sq.h, sq0 + 16)
                for m in range(2):
                    v.tensor_tensor(c2[:, m], c2[:, m], rep[:], AL.add).then_inc(se.h, 1)
            se.n += 2
        if DBG_CONVKV < 4:
            return

        emit_norm_rope(nc, io, cb, sems, src=c2, ntile=2, nheads=4, chunk=chunk,
                       kt=True, dst=qr, sq_t=sq_t, rsb=rsb, rbf=rbf, rbt=rbt,
                       cosb=cosb, ps_s=ps_s, ps_r=ps_r)

        with nc.Block() as blk:
            sw0 = sw.n

            @blk.sync
            def _(sy):
                for m in range(2):
                    sy.dma_start(out=sc["Ksh"][128 * m:128 * (m + 1), chunk * CH:(chunk + 1) * CH],
                                 in_=qr[:, m]).then_inc(sw.h, 16)
                sy.wait_ge(sw.h, sw0 + 32)
            sw.n += 32
        if DBG_CONVKV < 5:
            return

        with nc.Block() as blk:
            sm0, sa0 = sm.n, sa.n

            @blk.tensor
            def _(t):
                for tt in range(4):
                    for dtile in range(2):
                        k = 2 * tt + dtile
                        if k >= 1:
                            t.wait_ge(sa.h, sa0 + k)
                        colbase = HALO + 128 * tt - (1 if dtile == 1 else 0)
                        t.transpose(ps_t[:], qv[:, 10 + dtile, colbase:colbase + 128],
                                    cb["identity"][:]).then_inc(sm.h, 1)
            sm.n += 8

            @blk.scalar
            def _(s):
                for k in range(8):
                    tt, dtile = divmod(k, 2)
                    s.wait_ge(sm.h, sm0 + k + 1)
                    s.activation(vstage[:, tt, 128 * dtile:128 * (dtile + 1)],
                                 ps_t[:], AF.Copy).then_inc(sa.h, 1)
            sa.n += 8

            @blk.sync
            def _(sy):
                sy.wait_ge(sa.h, sa0 + 8)
                sw0v = sw.n
                sy.dma_start(out=sc["Vsh"][chunk * CH:(chunk + 1) * CH, :]
                             .rearrange("(tt p) d -> p tt d", p=128),
                             in_=vstage[:]).then_inc(sw.h, 16)
                sy.wait_ge(sw.h, sw0v + 16)
            sw.n += 16
        if DBG_CONVKV < 6:
            return

        if is_own:
            w2b = E(nc.sbuf_tensor(f"w2_{_uid()}", [128, 2, 3, 8, 128], BF16))
            emit_conv(nc, cb, sems, bd=cb["bdq"], w2=None, nbd=8, ni=8, src=qv, src_off=0,
                      src_col0=14, dst=c1, dstW=CW, dstcol0=14, bias=cb["qc1b"], ps=ps_a,
                      zero_cols=(14, 16) if zero_hist else None)
            if DBG_CONVKV < 7:
                return
            emit_qc2(nc, io, sc, cb, sems, c1, c2, w2b, ps_a)
            if DBG_CONVKV < 8:
                return
            with nc.Block() as blk:
                sm0, se0, sq0 = sm.n, se.n, sq.n

                @blk.tensor
                def _(t):
                    for th in range(2):
                        for m in range(2):
                            mm = t.matmul(ps_s[0:64, 256 * th:256 * th + 256],
                                          cb["halfsum"][:],
                                          qv[:, 8 + m, HALO + 256 * th:HALO + 256 * th + 256],
                                          start=(m == 0), stop=(m == 1))
                        mm.then_inc(sm.h, 1)
                sm.n += 2

                @blk.vector
                def _(v):
                    v.wait_ge(sm.h, sm0 + 2)
                    v.tensor_copy(rep[0:64, :], ps_s[0:64, 0:CH]).then_inc(se.h, 1)
                se.n += 1

                @blk.scalar
                def _(s):
                    s.wait_ge(se.h, se0 + 1)
                    s.dma_start(out=rep[64:128, :], in_=rep[0:64, :]).then_inc(sq.h, 16)
                sq.n += 16

                @blk.vector
                def _(v):
                    v.wait_ge(sq.h, sq0 + 16)
                    for m in range(8):
                        v.tensor_tensor(c2[:, m], c2[:, m], rep[:], AL.add).then_inc(se.h, 1)
                se.n += 8
            if DBG_CONVKV < 9:
                return

            emit_norm_rope(nc, io, cb, sems, src=c2, ntile=8, nheads=16, chunk=chunk,
                           kt=False, dst=qr, sq_t=sq_t, rsb=rsb, rbf=rbf, rbt=rbt,
                           cosb=cosb, ps_s=ps_s, ps_r=ps_r)
            if DBG_CONVKV < 10:
                return

            with nc.Block() as blk:
                sw0 = sw.n

                @blk.sync
                def _(sy):
                    sy.dma_start(out=sc["qsp"][own_idx]
                                 .rearrange("(m p) c -> p m c", p=128),
                                 in_=qr[:]).then_inc(sw.h, 16)
                    sy.wait_ge(sw.h, sw0 + 16)
                sw.n += 16


def emit_qc2(nc, io, sc, cb, sems, c1, c2, w2b, ps):
    sm, sa, se, sw = sems["sm"], sems["sa"], sems["se"], sems["sw"]
    q2p = sems["_stream"]("q2p")
    q2p0 = [q2p[0].n, q2p[1].n]
    with nc.Block() as blk:
        sm0, sa0, se0 = sm.n, sa.n, se.n

        @blk.sync
        def _(sy):
            for m in range(8):
                if m >= 2:
                    sy.wait_ge(sm.h, sm0 + m - 1)
                for j in range(3):
                    sy.dma_start(out=w2b[:, m % 2, j],
                                 in_=sc["qc2T_bf"][j].rearrange("(it p) o -> p it o", p=128)
                                     [:, :, 128 * m:128 * (m + 1)]).then_inc(q2p[m % 2].h, 16)
        for par in range(2):
            q2p[par].n += 48 * 4

        @blk.tensor
        def _(t):
            for m in range(8):
                t.wait_ge(q2p[m % 2].h, q2p0[m % 2] + 48 * (m // 2 + 1))
                if m >= 2:
                    t.wait_ge(se.h, se0 + m - 1)
                for th in range(2):
                    first = True
                    for i in range(8):
                        for j in range(3):
                            mm = t.matmul(ps[m % 2][:, 512 * th:512 * th + 256],
                                          w2b[:, m % 2, j, i],
                                          c1[:, i, HALO + j - 2 + 256 * th:HALO + j - 2 + 256 * th + 256],
                                          start=first, stop=(i == 7 and j == 2))
                            first = False
                mm.then_inc(sm.h, 1)
        sm.n += 8

        @blk.scalar
        def _(s):
            for m in range(8):
                s.wait_ge(sm.h, sm0 + m + 1)
                psv = ps[m % 2].rearrange("p (g c) -> p g c", g=2)[:, :, 0:256]
                ov = c2[:, m].rearrange("p (g c) -> p g c", g=2)
                s.activation(ov, psv, AF.Identity, bias=cb["qc2b"][:, m:m + 1]).then_inc(sa.h, 1)
        sa.n += 8

        @blk.vector
        def _(v):
            for m in range(8):
                v.wait_ge(sa.h, sa0 + m + 1)
                v.engine_nop().then_inc(se.h, 1)
        se.n += 8


def emit_barrier(nc, io, sc, cb, sems, role):
    par_val = role % 2

    with nc.Block() as blk, \
         nc.sbuf_tensor(f"flg{_uid()}", [1, 16], I32) as flg:
        sfl0 = sems["sfl"].n
        ssp = sems["ssp"]

        @blk.gpsimd
        def _(g):
            g.memset(flg[0:1, 0:1], 1)
            g.engine_nop().then_inc(sems["sfl"].h, 1)
            g.wait_ge(sems["sfl"].h, sfl0 + 1)
            g.dma_start(out=sc["flags"][0:1, par_val:par_val + 1], in_=flg[0:1, 0:1]).then_inc(ssp.h, 16)
            ssp.n += 16
            g.wait_ge(ssp.h, ssp.n)
            with g.register(f"r_t{_uid()}") as r_tgt, g.register(f"r_c{_uid()}") as r_cont, \
                 g.register(f"r_a{_uid()}") as r_a, g.register(f"r_b{_uid()}") as r_b, \
                 g.register(f"r_m{_uid()}") as r_m:
                g.reg_mov(r_tgt, ssp.n + 16)
                g.reg_mov(r_cont, 1)
                with g.While(r_cont):
                    g.dma_start(out=flg[0:1, 2:4], in_=sc["flags"][0:1, 0:2]).then_inc(ssp.h, 16)
                    g.wait_ge(ssp.h, r_tgt)
                    g.reg_alu(r_tgt, r_tgt, 16, op=AL.add)
                    g.reg_load(r_a, flg[0:1, 2:3])
                    g.reg_load(r_b, flg[0:1, 3:4])
                    g.reg_alu(r_m, r_a, r_b, op=AL.min)
                    g.reg_alu(r_cont, r_m, 1, op=AL.is_lt)
        sems["sfl"].n += 1



def emit_role(nc, io, sc, cb, sems, role):
    own, foreign = ROLE_TABLES[role]
    nslab = len(own) + len(foreign)
    slabs = list(range(nslab))
    chunks = own + foreign
    par = role % 2

    with nc.Block() as blk, nc.sbuf_tensor(f"zf{_uid()}", [1, 2], I32) as zflag:
        sgp = sems["sgp"]

        @blk.gpsimd
        def _(g):
            g.memset(zflag[:], 0).then_inc(sems["sfl"].h, 1)
            g.wait_ge(sems["sfl"].h, sems["sfl"].n + 1)
            g.dma_start(out=sc["flags"][0:1, par:par + 1], in_=zflag[0:1, 0:1]).then_inc(sgp.h, 16)
            g.wait_ge(sgp.h, sgp.n + 16)
        sems["sfl"].n += 1
        sgp.n += 16

    for g0 in range(0, nslab, 2):
        emit_ffn_group(nc, io, sc, cb, sems, slabs[g0:g0 + 2], mode="ffn1")

    if DBG_DUMP:
        with nc.Block() as blk:
            @blk.sync
            def _(sy):
                for sl in slabs:
                    sems["sst"].inc(sy.dma_start(out=io["dbg_x1"][sl], in_=sc["x1sp"][sl]))
                sy.wait_ge(sems["sst"].h, sems["sst"].n)
    if DBG_STOP == "A":
        emit_dummy_out(nc, io, sems)
        return

    groups = []
    for g0 in range(0, len(own), 2):
        groups.append((slabs[g0:g0 + 2][:len(own) - g0], True))
    for g0 in range(len(own), nslab, 2):
        groups.append((slabs[g0:g0 + 2], False))
    for grp, is_own_g in groups:
        with contextlib.ExitStack() as est:
            qv = emit_qkv_group(nc, io, sc, cb, sems, est, grp, is_own_g)
            if DBG_STOP != "B0":
                for gi, sl in enumerate(grp):
                    emit_convkv_slab(nc, io, sc, cb, sems, sl, chunks[sl], is_own_g,
                                     qv[gi], own_idx=(sl if is_own_g else None))

    if DBG_STOP not in ("B1", "B0"):
        emit_barrier(nc, io, sc, cb, sems, role)

    if DBG_DUMP and DBG_STOP != "B0":
        with nc.Block() as blk:
            lim = (M_A * CH) if role < 2 else S

            @blk.sync
            def _(sy):
                sems["sst"].inc(sy.dma_start(out=io["dbg_K"][:, 0:lim], in_=sc["Ksh"][:, 0:lim]))
                sems["sst"].inc(sy.dma_start(out=io["dbg_V"][0:lim], in_=sc["Vsh"][0:lim]))
                for j in range(len(own)):
                    sems["sst"].inc(sy.dma_start(out=io["dbg_q"][j], in_=sc["qsp"][j]))
                sy.wait_ge(sems["sst"].h, sems["sst"].n)
    if DBG_STOP in ("B", "B1", "B0"):
        emit_dummy_out(nc, io, sems)
        return

    emit_attention(nc, io, sc, cb, sems, role, own)
    if DBG_DUMP:
        with nc.Block() as blk:
            @blk.sync
            def _(sy):
                for j in range(len(own)):
                    sems["sst"].inc(sy.dma_start(out=io["dbg_o"][j], in_=sc["qsp"][j]))
                sy.wait_ge(sems["sst"].h, sems["sst"].n)
    if DBG_STOP == "C":
        emit_dummy_out(nc, io, sems)
        return

    emit_phaseD(nc, io, sc, cb, sems, role, own)


def emit_dummy_out(nc, io, sems):
    with nc.Block() as blk:
        @blk.sync
        def _(sy):
            sems["sst"].inc(sy.dma_start(out=io["out"][0, 0:128, :], in_=io["x_sl"][0, 0:128, 0:CH]))
            sy.wait_ge(sems["sst"].h, sems["sst"].n)


def emit_attention(nc, io, sc, cb, sems, role, own):
    NQ = len(own)
    KMAX_CH = max(own) + 1
    KT = KMAX_CH * 4
    sx, sm, se, sa, sq, sw = (sems[k] for k in ["sx", "sm", "se", "sa", "sq", "sw"])

    with contextlib.ExitStack() as est:
        E = est.enter_context
        ksb = E(nc.sbuf_tensor(f"ak_{_uid()}", [128, 4, KMAX_CH * CH], BF16))
        vON = E(nc.sbuf_tensor(f"av_{_uid()}", [128, KT, 260], BF16))
        qsb = E(nc.sbuf_tensor(f"aq_{_uid()}", [128, 8, NQ * CH], BF16))
        osb = E(nc.sbuf_tensor(f"ao_{_uid()}", [128, 8, NQ * CH], BF16))
        esb = E(nc.sbuf_tensor(f"ae_{_uid()}", [128, 2, 1024], BF16))
        dn = E(nc.sbuf_tensor(f"adn{_uid()}", [128, CH], F32))
        dnb = E(nc.sbuf_tensor(f"adb{_uid()}", [64, CH], F32))
        ostg = E(nc.sbuf_tensor(f"ag_{_uid()}", [64, 2, CH], BF16))
        ps_sc = [E(nc.psum_tensor(f"aps{_uid()}", [128, 1024], F32)) for _ in range(2)]
        ps_o = [E(nc.psum_tensor(f"apo{_uid()}", [65, 512], F32)) for _ in range(2)]
        ps_d = E(nc.psum_tensor(f"apd{_uid()}", [64, 512], F32))

        with nc.Block() as blk:
            sx0 = sx.n

            @blk.sync
            def _(sy):
                for kv in range(4):
                    for half in range(2):
                        sy.dma_start(out=ksb[64 * half:64 * half + 64, kv],
                                     in_=sc["Ksh"][64 * kv:64 * kv + 64, 0:KMAX_CH * CH]).then_inc(sx.h, 16)
                for t in range(KT):
                    sy.dma_start(out=vON[:, t].rearrange("p (kv c) -> p kv c", c=65)[:, :, 0:64],
                                 in_=sc["Vsh"][128 * t:128 * (t + 1), :]
                                     .rearrange("p (kv c) -> p kv c", c=64)).then_inc(sx.h, 16)
                for j in range(NQ):
                    sy.dma_start(out=qsb[:, :, j * CH:(j + 1) * CH],
                                 in_=sc["qsp"][j].rearrange("(m p) c -> p m c", p=128)).then_inc(sx.h, 16)
                sy.wait_ge(sx.h, sx0 + 16 * (8 + KT + NQ))

            @blk.gpsimd
            def _(g):
                g.memset(vON[:, :, :].rearrange("p n (kv c) -> p n kv c", c=65)[:, :, :, 64:65],
                         1.0).then_inc(sems["sfl"].h, 1)
            sems["sfl"].n += 1
        sx.n += 16 * (8 + KT + NQ)

        for ci, c in enumerate(own):
            cq = ci * CH
            NU = (c + 1) * 2
            UL = [(cp, kp) for cp in range(c + 1) for kp in range(2)]
            with nc.Block() as blk:
                sm0, sa0, se0, sq0 = sm.n, sa.n, se.n, sq.n
                SMh = 3 * NU + 2
                SAh = 2 * NU + 2
                SEh = 8

                def sm_qk(hp, u):
                    return sm0 + hp * SMh + 3 * u + 1

                def sm_av(hp, u, sl):
                    return sm0 + hp * SMh + 3 * u + 2 + sl

                def sm_bc(hp, sl):
                    return sm0 + hp * SMh + 3 * NU + sl + 1

                def sa_exp(hp, u, sl):
                    return sa0 + hp * SAh + 2 * u + sl + 1

                def sa_div(hp, sl):
                    return sa0 + hp * SAh + 2 * NU + sl + 1

                def se_max(hp, sl):
                    return se0 + hp * SEh + 4 + 2 * sl + 1

                def se_div(hp, sl):
                    return se0 + hp * SEh + 4 + 2 * sl + 2

                def se_mk(hp, kp, sl):
                    return se0 + hp * SEh + 2 * kp + sl + 1

                @blk.tensor
                def _(t):
                    t.wait_ge(sems["sfl"].h, sems["sfl"].n)
                    for hp in range(8):
                        h0, h1 = 2 * hp, 2 * hp + 1
                        for u, (cp, kp) in enumerate(UL):
                            g = hp * NU + u
                            if g >= 1:
                                ph, pu = (hp, u - 1) if u >= 1 else (hp - 1, NU - 1)
                                t.wait_ge(sa.h, sa_exp(ph, pu, 1))
                            if u == 0 and hp >= 1:
                                t.wait_ge(se.h, se_div(hp - 1, 1))
                            for sl, h in ((0, h0), (1, h1)):
                                kv = h // 4
                                for e in range(2):
                                    mm = t.matmul(
                                        ps_sc[sl][:, 512 * e:512 * e + 512],
                                        ksb[64 * (h % 2):64 * (h % 2) + 64, kv,
                                            CH * cp + 128 * (2 * kp + e):
                                            CH * cp + 128 * (2 * kp + e) + 128],
                                        qsb[64 * (h % 2):64 * (h % 2) + 64, h // 2, cq:cq + CH],
                                        start=True, stop=True, skip_group_check=True)
                            mm.then_inc(sm.h, 1)
                            diag = (cp == c)
                            for sl, h in ((0, h0), (1, h1)):
                                kv = h // 4
                                if diag:
                                    t.wait_ge(se.h, se_mk(hp, kp, sl))
                                else:
                                    t.wait_ge(sa.h, sa_exp(hp, u, sl))
                                for e in range(2):
                                    ktile = 4 * cp + 2 * kp + e
                                    mm = t.matmul(
                                        ps_o[sl][:, :],
                                        vON[:, ktile, 65 * kv:65 * kv + 65],
                                        esb[:, sl, 512 * e:512 * e + 512],
                                        start=(u == 0 and e == 0), stop=(u == NU - 1 and e == 1))
                                mm.then_inc(sm.h, 1)
                        for sl in range(2):
                            t.wait_ge(sa.h, sa_div(hp, sl))
                            t.matmul(ps_d[:, :], cb["ones_f32"][64:65, 0:64], dn[64:65, :],
                                     start=True, stop=True).then_inc(sm.h, 1)
                sm.n += 8 * SMh

                @blk.scalar
                def _(s):
                    for hp in range(8):
                        for u, (cp, kp) in enumerate(UL):
                            g = hp * NU + u
                            for sl in range(2):
                                if g >= 1:
                                    ph, pu = divmod(g - 1, NU)
                                    s.wait_ge(sm.h, sm_av(ph, pu, sl))
                                s.wait_ge(sm.h, sm_qk(hp, u))
                                s.activation(esb[:, sl], ps_sc[sl][:, :], AF.Exp,
                                             bias=cb["negC"][:], scale=0.125).then_inc(sa.h, 1)
                        for sl in range(2):
                            s.wait_ge(se.h, se_max(hp, sl))
                            chain(s, sems["ca"], [
                                lambda: s.activation(dn[64:65, :], dn[64:65, :], AF.Ln),
                                lambda: s.activation(dn[64:65, :], dn[64:65, :], AF.Exp, scale=-1.0),
                            ]).then_inc(sa.h, 1)
                sa.n += 8 * SAh

                omv = sems["_stream"]("omv")
                OMV0 = [omv[0].n, omv[1].n]

                @blk.vector
                def _(v):
                    for hp in range(8):
                        h0, h1 = 2 * hp, 2 * hp + 1
                        for kp in range(2):
                            u = NU - 2 + kp
                            for sl in range(2):
                                v.wait_ge(sa.h, sa_exp(hp, u, sl))
                                v.tensor_tensor(esb[:, sl], esb[:, sl],
                                                cb["trimask"][:, 1024 * kp:1024 * kp + 1024],
                                                AL.mult).then_inc(se.h, 1)
                        for sl, h in ((0, h0), (1, h1)):
                            v.wait_ge(sm.h, sm_av(hp, NU - 1, sl))
                            v.tensor_scalar_max(dn[64:65, :], ps_o[sl][64:65, :], 1e-30).then_inc(se.h, 1)
                            v.wait_ge(sm.h, sm_bc(hp, sl))
                            ops = [lambda: v.tensor_copy(dnb[:], ps_d[:, :])]
                            if h % 2 == 0:
                                ops.append(lambda: v.tensor_tensor(
                                    osb[0:64, h // 2, cq:cq + CH], ps_o[sl][0:64, :], dnb[:], AL.mult))
                            else:
                                if hp >= 2:
                                    v.wait_ge(omv[hp % 2].h, OMV0[hp % 2] + 16 * ((hp - 2) // 2 + 1))
                                ops.append(lambda: v.tensor_tensor(
                                    ostg[:, hp % 2], ps_o[sl][0:64, :], dnb[:], AL.mult))
                            chain(v, sems["cv"], ops).then_inc(se.h, 1)
                sems["se"].n += 8 * SEh

                omv0 = OMV0

                @blk.sync
                def _(sy):
                    for hp in range(8):
                        h1 = 2 * hp + 1
                        sy.wait_ge(se.h, se_div(hp, 1))
                        sy.dma_start(out=osb[64:128, h1 // 2, cq:cq + CH],
                                     in_=ostg[:, hp % 2]).then_inc(omv[hp % 2].h, 16)
                    for par in range(2):
                        sy.wait_ge(omv[par].h, omv0[par] + 16 * 4)
                for par in range(2):
                    omv[par].n += 16 * 4

        with nc.Block() as blk:
            sw0 = sw.n

            @blk.sync
            def _(sy):
                for j in range(NQ):
                    sy.dma_start(out=sc["qsp"][j].rearrange("(m p) c -> p m c", p=128),
                                 in_=osb[:, :, j * CH:(j + 1) * CH]).then_inc(sw.h, 16)
                sy.wait_ge(sw.h, sw0 + 16 * NQ)
            sw.n += 16 * NQ


def emit_phaseD(nc, io, sc, cb, sems, role, own):
    NQ = len(own)
    sx, sm, se, sa, sq, sw = (sems[k] for k in ["sx", "sm", "se", "sa", "sq", "sw"])

    with contextlib.ExitStack() as est, nc.Block() as blk:
        E = est.enter_context
        ob2 = E(nc.sbuf_tensor(f"do_{_uid()}", [128, 8, NQ * CH], BF16))
        wocol = E(nc.sbuf_tensor(f"dw_{_uid()}", [128, 2, 8, 128], BF16))
        x1t = E(nc.sbuf_tensor(f"dx_{_uid()}", [128, 2, CH], BF16))
        x2t = E(nc.sbuf_tensor(f"dy_{_uid()}", [128, 2, CH], BF16))
        ps_w = [E(nc.psum_tensor(f"dp_{_uid()}", [128, 512], F32)) for _ in range(2)]
        sx0, sm0, se0 = sx.n, sm.n, se.n
        wop = sems["_stream"]("wop")
        wop0 = [wop[0].n, wop[1].n]
        x1p = sems["_stream"]("x1p")
        x1p0 = [x1p[0].n, x1p[1].n]
        x2p = sems["_stream"]("x2p")
        x2p0 = [x2p[0].n, x2p[1].n]

        @blk.sync
        def _(sy):
            for j in range(NQ):
                sy.dma_start(out=ob2[:, :, j * CH:(j + 1) * CH],
                             in_=sc["qsp"][j].rearrange("(m p) c -> p m c", p=128)).then_inc(sx.h, 16)
            for ob in range(32):
                if ob >= 2:
                    sy.wait_ge(sm.h, sm0 + (ob - 1) * NQ)
                sy.dma_start(out=wocol[:, ob % 2],
                             in_=sc["woT_bf"].rearrange("(it p) o -> p it o", p=128)
                                 [:, :, 128 * ob:128 * (ob + 1)]).then_inc(wop[ob % 2].h, 16)
        sx.n += 16 * NQ
        for par in range(2):
            wop[par].n += 16 * 16

        @blk.gpsimd
        def _(g):
            k = 0
            for ob in range(32):
                for u in range(NQ):
                    g.wait_ge(se.h, se0 + k + 1)
                    g.dma_start(out=sc["x2sp"][u, 128 * ob:128 * (ob + 1), :],
                                in_=x2t[:, k % 2]).then_inc(x2p[k % 2].h, 16)
                    k += 1
            for par in range(2):
                g.wait_ge(x2p[par].h, x2p0[par] + 16 * (32 * NQ // 2))
        for par in range(2):
            x2p[par].n += 16 * (32 * NQ // 2)

        @blk.scalar
        def _(s):
            k = 0
            for ob in range(32):
                for u in range(NQ):
                    if k >= 2:
                        s.wait_ge(se.h, se0 + k - 1)
                    s.dma_start(out=x1t[:, k % 2],
                                in_=sc["x1sp"][u, 128 * ob:128 * (ob + 1), HALO:]).then_inc(x1p[k % 2].h, 16)
                    k += 1
        for par in range(2):
            x1p[par].n += 16 * (32 * NQ // 2)

        @blk.tensor
        def _(t):
            t.wait_ge(sx.h, sx0 + 16 * NQ)
            k = 0
            for ob in range(32):
                t.wait_ge(wop[ob % 2].h, wop0[ob % 2] + 16 * (ob // 2 + 1))
                for u in range(NQ):
                    if k >= 2:
                        t.wait_ge(se.h, se0 + k - 1)
                    for i in range(8):
                        mm = t.matmul(ps_w[k % 2][:, :], wocol[:, ob % 2, i],
                                      ob2[64 * 0:128, i, u * CH:(u + 1) * CH] if False else
                                      ob2[:, i, u * CH:(u + 1) * CH],
                                      start=(i == 0), stop=(i == 7))
                    mm.then_inc(sm.h, 1)
                    k += 1
        sm.n += 32 * NQ

        @blk.vector
        def _(v):
            k = 0
            for ob in range(32):
                for u in range(NQ):
                    v.wait_ge(sm.h, sm0 + k + 1)
                    v.wait_ge(x1p[k % 2].h, x1p0[k % 2] + 16 * (k // 2 + 1))
                    if k >= 2:
                        v.wait_ge(x2p[k % 2].h, x2p0[k % 2] + 16 * ((k - 2) // 2 + 1))
                    v.tensor_tensor(x2t[:, k % 2], ps_w[k % 2][:, :], x1t[:, k % 2],
                                    AL.add).then_inc(se.h, 1)
                    k += 1
        se.n += 32 * NQ


    for g0 in range(0, NQ, 2):
        grp = list(range(g0, min(g0 + 2, NQ)))
        emit_ffn_group(nc, io, sc, cb, sems, grp, mode="ffn2", own_idx=grp)


def emit_dummy_out(nc, io, sems):
    with nc.Block() as blk:
        @blk.sync
        def _(sy):
            sems["sst"].inc(sy.dma_start(out=io["out"][0, 0:128, :], in_=io["x_sl"][0, 0:128, 0:CH]))
            sy.wait_ge(sems["sst"].h, sems["sst"].n)


